# version 03
# baseline (speedup 1.0000x reference)
import os
import numpy as np
import ml_dtypes
from contextlib import ExitStack

KDBG = os.environ.get("KDBG", "")
KATN = int(os.environ.get("KATN", "100000"))
KATM = os.environ.get("KATM", "full")

import concourse.bass as bass
import concourse.mybir as mybir
from concourse.alu_op_type import AluOpType
from concourse.bass_utils import run_bass_kernel_spmd

BF16NP = ml_dtypes.bfloat16
F32 = mybir.dt.float32
BF16 = mybir.dt.bfloat16
I32 = mybir.dt.int32
AF = mybir.ActivationFunctionType
A = AluOpType
ET = mybir.EngineType

HID = 4096
SEQ = 2048
NH = 32
NKV = 8
HD = 128
BLOCK = 64
LOCAL = 16
VERT = 8
NB = SEQ // BLOCK
B_SPLIT = 20
TSPLIT = B_SPLIT * BLOCK
OWNW = (TSPLIT, SEQ - TSPLIT)
EXP_SHIFT = 12.0
MASK_NEG = 30000.0

PERM = np.array([8 * (i % 512) + (i // 512) for i in range(HID)], dtype=np.int64)


def _attended(i, j):
    return j <= i and ((i - j < LOCAL) or ((j + 1) % VERT == 0))


def _blist(i):
    return [j for j in range(NB) if _attended(i, j)]


def _chunks(role):
    if role == 0:
        return [(0, 640, True), (640, 1280, True)]
    return [(0, 512, False), (512, 1024, False), (1024, 1280, False),
            (1280, 1792, True), (1792, 2048, True)]


def _tc(w, step=512):
    return [(a, min(a + step, w)) for a in range(0, w, step)]


class Sem:
    __slots__ = ("h", "n", "_cm")

    def __init__(self, nc, name):
        self._cm = nc.semaphore(name)
        self.h = self._cm.__enter__()
        self.n = 0

    def inc(self, instr, k=1):
        instr.then_inc(self.h, k)
        self.n += k
        return self.n


class E:
    pass


def barrier(nc):
    with nc.Block() as blk:
        @blk.gpsimd
        def _(g):
            g.nop()

        @blk.sync
        def _(s):
            s.nop()

        @blk.tensor
        def _(t):
            t.nop()

        @blk.vector
        def _(v):
            v.nop()

        @blk.scalar
        def _(s):
            s.nop()



def build_program():
    nc = bass.Bass("TRN2", target_bir_lowering=False)
    e = E()
    e.nc = nc

    din = {}
    for name, shape, dt in (
        ("xT", [HID, SEQ], F32), ("pwT", [512, HID], I32), ("scaleT", [32, HID], F32),
        ("wqkvT", [HID, 6144], F32), ("wdT", [HID, HID], F32),
        ("bffn", [128, 32], F32), ("bqkv", [128, 48], F32), ("bd", [128, 32], F32),
        ("cosq", [128, SEQ], BF16), ("sinq", [128, SEQ], BF16),
        ("cosk", [128, SEQ], BF16), ("sink", [128, SEQ], BF16),
        ("pmat", [128, 128], BF16), ("ident", [128, 128], BF16),
        ("mdiag", [128, 128], F32),
        ("onesc", [128, 1], BF16), ("onesr", [1, 128], BF16), ("onesrf", [1, 128], F32),
        ("onesm", [128, 128], BF16),
        ("par", [1, 1], I32),
    ):
        din[name] = nc.dram_tensor(name, shape, dt, kind="ExternalInput")
    y = nc.dram_tensor("y", [HID, OWNW[0]], F32, kind="ExternalOutput")
    e.d = {k: v.ap() for k, v in din.items()}
    e.d["y"] = y.ap()

    scr = {}
    for name, shape, dt in (
        ("wt2d", [HID, HID], BF16), ("ncol_b", [32 * 128], BF16),
        ("r1s", [128, 32, OWNW[0]], BF16), ("xns", [128, 32, OWNW[0]], BF16),
        ("r2s", [128, 32, OWNW[0]], BF16), ("qs", [NH, 128, OWNW[0]], BF16),
    ):
        scr[name] = nc.dram_tensor(name, shape, dt, kind="Internal")
    e.s = {k: v.ap() for k, v in scr.items()}

    with ExitStack() as top:
        ent = top.enter_context
        e.s_w = [ent(nc.sbuf_tensor(f"s_w{i}", [128, 32 * 128], BF16)) for i in range(2)]
        e.s_par = ent(nc.sbuf_tensor("s_par", [1, 1], I32))
        e.s_ones = ent(nc.sbuf_tensor("s_ones", [128, 1], BF16))
        e.s_onesr = ent(nc.sbuf_tensor("s_onesr", [1, 128], BF16))
        e.s_onesrf = ent(nc.sbuf_tensor("s_onesrf", [1, 128], F32))
        e.s_onesm = ent(nc.sbuf_tensor("s_onesm", [128, 128], BF16))
        e.s_pmat = ent(nc.sbuf_tensor("s_pmat", [128, 128], BF16))
        e.s_ident = ent(nc.sbuf_tensor("s_ident", [128, 128], BF16))
        e.s_mdiag = ent(nc.sbuf_tensor("s_mdiag", [128, 128], F32))
        e.s_bffn = ent(nc.sbuf_tensor("s_bffn", [128, 32], F32))
        e.s_bqkv = ent(nc.sbuf_tensor("s_bqkv", [128, 48], F32))
        e.s_bd = ent(nc.sbuf_tensor("s_bd", [128, 32], F32))
        e.s_ncol = ent(nc.sbuf_tensor("s_ncol", [1, HID], BF16))
        e.s_tab = {k: ent(nc.sbuf_tensor(f"s_{k}", [128, 640], BF16))
                   for k in ("cos", "sin")}
        e.s_row = {k: ent(nc.sbuf_tensor(f"s_row_{k}", [1, 640], F32))
                   for k in ("m", "msq", "var", "rstd", "tmp")}
        e.s_mbf = ent(nc.sbuf_tensor("s_mbf", [1, 640], BF16))
        e.s_Rb = ent(nc.sbuf_tensor("s_Rb", [128, 640], F32))
        e.s_M2 = ent(nc.sbuf_tensor("s_M2", [128, 640], F32))
        e.s_R2 = ent(nc.sbuf_tensor("s_R2", [128, 640], F32))
        e.s_t1 = [ent(nc.sbuf_tensor(f"s_t1_{i}", [128, 512], F32)) for i in range(2)]
        e.s_t2 = [ent(nc.sbuf_tensor(f"s_t2_{i}", [128, 512], F32)) for i in range(2)]
        e.s_tb = [ent(nc.sbuf_tensor(f"s_tb_{i}", [128, 512], BF16)) for i in range(2)]
        e.s_m1 = ent(nc.sbuf_tensor("s_m1", [128, 512], BF16))
        e.s_mv = ent(nc.sbuf_tensor("s_mv", [128, 640], F32))
        e.s_sq = ent(nc.sbuf_tensor("s_sq", [128, 640], BF16))
        e.s_qh = [ent(nc.sbuf_tensor(f"s_qh{i}", [128, 128], BF16)) for i in range(2)]
        e.s_probs = [ent(nc.sbuf_tensor(f"s_probs{i}", [128, 10 * 128], BF16))
                     for i in range(2)]
        e.s_zi = ent(nc.sbuf_tensor("s_zi", [1, 128], F32))
        e.s_zib = ent(nc.sbuf_tensor("s_zib", [1, 128], BF16))
        e.s_zb = [ent(nc.sbuf_tensor(f"s_zb{i}", [128, 128], F32)) for i in range(2)]
        e.s_out = [ent(nc.sbuf_tensor(f"s_out{i}", [128, 512], F32)) for i in range(2)]
        e.s_rr = [ent(nc.sbuf_tensor(f"s_rr{i}", [128, 512], BF16)) for i in range(2)]

        e.s_c_eps = ent(nc.sbuf_tensor("s_c_eps", [128, 1], F32))
        e.s_c_sh = ent(nc.sbuf_tensor("s_c_sh", [128, 1], F32))
        nc.const_aps.register(F32, 1e-5, e.s_c_eps[:, :])
        nc.const_aps.register(F32, -EXP_SHIFT, e.s_c_sh[:, :])

        e._sems = []
        for nm in ("dg", "dy", "qd", "wf", "gt", "pf", "vv", "ss", "tt", "os"):
            sm = Sem(nc, "sem_" + nm)
            setattr(e, nm, sm)
            e._sems.append(sm)

        _prelude(e)
        barrier(nc)

        if KDBG.startswith("PRE"):
            return nc
        if KDBG == "P0":
            g = nc.gpsimd
            e.dg.inc(g.dma_start(e.d["y"][:, 0:1280], e.s["wt2d"][:, 0:1280]), 16)
            g.wait_ge(e.dg.h, e.dg.n)
            barrier(nc)
            return nc

        parity = nc.values_load(
            e.s_par[0:1, 0:1],
            engines=[ET.Pool, ET.DVE, ET.Activation, ET.PE, ET.SP],
            skip_runtime_bounds_check=True)
        snap = [sm.n for sm in e._sems]
        for role in nc.Switch(engines=[nc.gpsimd, nc.vector, nc.scalar,
                                       nc.tensor, nc.sync],
                              index=[parity] * 5, n=2):
            for sm, v in zip(e._sems, snap):
                sm.n = v
            _role_body(e, role)
    return nc



def _prelude(e):
    nc = e.nc
    g, v, sc, t, sy = nc.gpsimd, nc.vector, nc.scalar, nc.tensor, nc.sync

    g.memset(e.s_c_eps[:, :], 1e-5)
    g.memset(e.s_c_sh[:, :], -EXP_SHIFT)
    for name, dst in (("par", e.s_par), ("onesc", e.s_ones), ("onesr", e.s_onesr),
                      ("onesrf", e.s_onesrf), ("onesm", e.s_onesm), ("pmat", e.s_pmat), ("ident", e.s_ident),
                      ("mdiag", e.s_mdiag), ("bffn", e.s_bffn), ("bqkv", e.s_bqkv),
                      ("bd", e.s_bd)):
        e.dg.inc(g.dma_start(dst[:, :], e.d[name]), 16)
    g.wait_ge(e.dg.h, e.dg.n)
    barrier(nc)
    v.tensor_scalar(e.s_bffn[:, :], e.s_bffn[:, :], 0.5, None, A.mult)
    barrier(nc)

    if KDBG == "PRE1":
        e.dg.inc(g.dma_start(e.d["y"][0:1, 0:1], e.s_par[0:1, 0:1].bitcast(F32)[:, 0:1]), 16)
        g.wait_ge(e.dg.h, e.dg.n)
        barrier(nc)
        return

    with (
        nc.sbuf_tensor("dq_pw", [128, HID], I32) as s_pw,
        nc.sbuf_tensor("dq_sc", [128, HID], F32) as s_sc,
        nc.sbuf_tensor("dq_nib", [128, HID], I32) as s_nib,
        nc.sbuf_tensor("dq_out", [128, HID], BF16) as s_wout,
        nc.sbuf_tensor("dq_cse", [128, 32], F32) as s_cse,
        nc.sbuf_tensor("dq_csb", [128, 128], BF16) as s_csb,
        nc.sbuf_tensor("dq_csr", [32, 128], BF16) as s_csr,
        nc.psum_tensor("dq_pcol", [128, 32], F32) as p_col,
        nc.psum_tensor("dq_ptr", [128, 128], BF16) as p_tr,
    ):
        g.memset(s_cse[:, :], 0.0)
        barrier(nc)
        if KDBG == "PRE1B":
            e.dg.inc(g.dma_start(e.d["y"][0:1, 0:1], e.s_par[0:1, 0:1].bitcast(F32)[:, 0:1]), 16)
            g.wait_ge(e.dg.h, e.dg.n)
            barrier(nc)
            return
        for wt in range(4):
            g.wait_ge(e.vv.h, e.vv.n)
            e.dg.inc(g.dma_start(s_pw[:, :], e.d["pwT"][wt * 128:(wt + 1) * 128, :]), 16)
            for j in range(8):
                e.dg.inc(g.dma_start(
                    s_sc[16 * j:16 * (j + 1), :],
                    e.d["scaleT"][8 * wt + j:8 * wt + j + 1, :].partition_broadcast(16)), 16)
            g.wait_ge(e.dg.h, e.dg.n)
            barrier(nc)
            for ee in range(8):
                kslab = ee * 4 + wt
                if KDBG != "PRE2A":
                    e.vv.inc(v.tensor_scalar(s_nib[:, :], s_pw[:, :], 4 * ee, 28,
                                             A.arith_shift_left, A.arith_shift_right))
                    v.wait_ge(e.vv.h, e.vv.n)
                if KDBG not in ("PRE2A", "PRE2B"):
                    e.vv.inc(v.tensor_tensor(s_wout[:, :], s_nib[:, :], s_sc[:, :], A.mult))
                if KDBG not in ("PRE2A", "PRE2B", "PRE2C"):
                    sy.wait_ge(e.vv.h, e.vv.n)
                    e.dy.inc(sy.dma_start(e.s["wt2d"][kslab * 128:(kslab + 1) * 128, :],
                                          s_wout[:, :]), 16)
                if KDBG not in ("PRE2A", "PRE2B", "PRE2C", "PRE2D"):
                    t.wait_ge(e.vv.h, e.vv.n)
                    for oc in range(32):
                        mm = t.matmul(p_col[:, oc:oc + 1], s_wout[:, oc * 128:(oc + 1) * 128],
                                      e.s_ones[:, :], start=True, stop=True,
                                      skip_group_check=True)
                    e.tt.inc(mm)
                if KDBG not in ("PRE2A", "PRE2B", "PRE2C", "PRE2D", "PRE2E", "PRE2G"):
                    v.wait_ge(e.tt.h, e.tt.n)
                    e.vv.inc(v.tensor_tensor(s_cse[:, :], s_cse[:, :], p_col[:, :], A.add))
                if KDBG not in ("PRE2A", "PRE2B", "PRE2C", "PRE2D", "PRE2E", "PRE2F"):
                    v.wait_ge(e.dy.h, e.dy.n)
                if KDBG.startswith("PRE2"):
                    break
            if KDBG.startswith("PRE2"):
                break
        barrier(nc)
        if KDBG.startswith("PRE2"):
            e.dg.inc(g.dma_start(e.d["y"][0:1, 0:1], e.s_par[0:1, 0:1].bitcast(F32)[:, 0:1]), 16)
            g.wait_ge(e.dg.h, e.dg.n)
            barrier(nc)
            return
        if KDBG == "PRE3":
            e.dg.inc(g.dma_start(e.d["y"][0:1, 0:1], e.s_par[0:1, 0:1].bitcast(F32)[:, 0:1]), 16)
            g.wait_ge(e.dg.h, e.dg.n)
            barrier(nc)
            return

        g.memset(s_csb[:, :], 0.0)
        barrier(nc)
        v.tensor_scalar(s_csb[:, 0:32], s_cse[:, :], -1.0, None, A.mult)
        barrier(nc)
        e.tt.inc(t.transpose(p_tr[:, :], s_csb[:, :], e.s_ident[:, :]))
        barrier(nc)
        sc.copy(s_csr[:, :], p_tr[0:32, :])
        barrier(nc)
        e.dg.inc(g.dma_start(e.s["ncol_b"][:], s_csr[:, :]), 16)
        g.wait_ge(e.dg.h, e.dg.n)
        e.dg.inc(g.dma_start(e.s_ncol[:, :],
                             e.s["ncol_b"][:].rearrange("(a b) -> a b", a=1)), 16)
        g.wait_ge(e.dg.h, e.dg.n)
        sy.wait_ge(e.dy.h, e.dy.n)
        barrier(nc)



def _stats(e, src_tiles, w, p_sum, p_sq):
    nc = e.nc
    v, sc, t = nc.vector, nc.scalar, nc.tensor
    tcs = _tc(w)
    ss0 = e.ss.n
    tt0 = e.tt.n
    for kt in range(32):
        e.ss.inc(sc.activation(e.s_sq[:, 0:w], src_tiles(kt), AF.Square))
        sc.wait_ge(e.tt.h, tt0 + kt + 1)
    for kt in range(32):
        for c, (a, b) in enumerate(tcs):
            t.matmul(p_sum[c][0:1, 0:b - a], e.s_ones[:, :], src_tiles(kt)[:, a:b],
                     start=(kt == 0), stop=(kt == 31), skip_group_check=True)
        t.wait_ge(e.ss.h, ss0 + kt + 1)
        for c, (a, b) in enumerate(tcs):
            mm = t.matmul(p_sq[c][0:1, 0:b - a], e.s_ones[:, :], e.s_sq[:, a:b],
                          start=(kt == 0), stop=(kt == 31), skip_group_check=True)
        e.tt.inc(mm)
    barrier(nc)
    r = e.s_row
    for c, (a, b) in enumerate(tcs):
        v.tensor_scalar(r["m"][:, a:b], p_sum[c][0:1, 0:b - a], 1.0 / HID, None, A.mult)
        v.tensor_scalar(r["msq"][:, a:b], p_sq[c][0:1, 0:b - a], 1.0 / HID, None, A.mult)
    e.vv.inc(v.nop())
    v.wait_ge(e.vv.h, e.vv.n)
    e.vv.inc(v.tensor_tensor(r["tmp"][:, 0:w], r["m"][:, 0:w], r["m"][:, 0:w], A.mult))
    v.wait_ge(e.vv.h, e.vv.n)
    e.vv.inc(v.tensor_tensor(r["var"][:, 0:w], r["msq"][:, 0:w], r["tmp"][:, 0:w],
                             A.subtract))
    v.wait_ge(e.vv.h, e.vv.n)
    e.vv.inc(v.tensor_copy(e.s_mbf[:, 0:w], r["m"][:, 0:w]))
    sc.wait_ge(e.vv.h, e.vv.n)
    e.ss.inc(sc.activation(r["tmp"][:, 0:w], r["var"][:, 0:w], AF.Sqrt, bias=1e-5))
    v.wait_ge(e.ss.h, e.ss.n)
    e.vv.inc(v.reciprocal(r["rstd"][:, 0:w], r["tmp"][:, 0:w]))
    barrier(nc)


def _bcast(e, dst, row, w, p_bc, scale=None):
    nc = e.nc
    v, sc, t = nc.vector, nc.scalar, nc.tensor
    src = row
    if scale is not None:
        v.tensor_scalar(e.s_row["tmp"][:, 0:w], row, scale, None, A.mult)
        e.vv.inc(nc.vector.nop())
        t.wait_ge(e.vv.h, e.vv.n)
        src = e.s_row["tmp"][:, 0:w]
    for (a, b) in _tc(w):
        e.tt.inc(t.matmul(p_bc[:, 0:b - a], e.s_onesrf[:, :], src[0:1, a:b],
                          start=True, stop=True, skip_group_check=True))
        sc.wait_ge(e.tt.h, e.tt.n)
        e.ss.inc(sc.copy(dst[:, a:b], p_bc[:, 0:b - a]))
        t.wait_ge(e.ss.h, e.ss.n)
    barrier(nc)


def gemm(e, *, n_out, rhs, wsrc, weng, p_g, tcs, rank1=None, epilogue):
    nc = e.nc
    t = nc.tensor
    nch = len(tcs)
    gt0, pf0 = e.gt.n, e.pf.n
    for i in range(n_out):
        buf = i % 2
        eng = nc.sync if weng == "s" else nc.gpsimd
        sem = e.dy if weng == "s" else e.dg
        if i >= 2:
            eng.wait_ge(e.gt.h, gt0 + (i - 1) * nch)
        sem.inc(eng.dma_start(e.s_w[buf][:, :].rearrange("p (k o) -> p k o", k=32), wsrc(i)), 16)
        t.wait_ge(sem.h, sem.n)
        for c, (a, b) in enumerate(tcs):
            slot = (i * nch + c) % 2
            pap = p_g[slot][:, 0:b - a]
            need = i * nch + c - 1
            if need >= 0:
                t.wait_ge(e.pf.h, pf0 + need)
            first = True
            if rank1 is not None:
                t.matmul(pap, rank1[0](i), rank1[1](a, b), start=True, stop=False,
                         skip_group_check=True)
                first = False
            for kt in range(32):
                mm = t.matmul(pap, e.s_w[buf][:, kt * 128:(kt + 1) * 128], rhs(kt, a, b),
                              start=first, stop=(kt == 31), skip_group_check=True)
                first = False
            e.gt.inc(mm)
        for c, (a, b) in enumerate(tcs):
            slot = (i * nch + c) % 2
            tgt = gt0 + i * nch + c + 1
            nc.vector.wait_ge(e.gt.h, tgt)
            nc.scalar.wait_ge(e.gt.h, tgt)
            epilogue(i, c, a, b, p_g[slot][:, 0:b - a], (i * nch + c) % 2)
    barrier(nc)



def _role_body(e, role):
    nc = e.nc
    own0, own1 = (0, TSPLIT) if role == 0 else (TSPLIT, SEQ)
    ownw = own1 - own0
    keyw = own1

    with ExitStack() as st:
        ent = st.enter_context
        s_kT = ent(nc.sbuf_tensor(f"r{role}_kT", [128, NKV * keyw], BF16))
        s_vT = ent(nc.sbuf_tensor(f"r{role}_vT", [128, NKV * keyw], BF16))
        p_g = [ent(nc.psum_tensor(f"r{role}_pg{i}", [128, 512], F32)) for i in range(2)]

        def dump3d(scr_ap, wdump):
            gg = nc.gpsimd
            e.dg.inc(gg.dma_start(
                e.d["y"][:, 0:wdump].rearrange("(k p) t -> p k t", p=128),
                scr_ap), 16)
            gg.wait_ge(e.dg.h, e.dg.n)
            barrier(nc)

        bw = 640 if role == 0 else 512
        with (nc.sbuf_tensor(f"r{role}_B1", [128, 32 * bw], BF16) as s_B1,
              nc.sbuf_tensor(f"r{role}_B2", [128, 32 * bw], BF16) as s_B2,
              nc.psum_tensor(f"r{role}_psum0", [1, 512], F32) as ps0,
              nc.psum_tensor(f"r{role}_psum1", [1, 512], F32) as ps1,
              nc.psum_tensor(f"r{role}_psq0", [1, 512], F32) as pq0,
              nc.psum_tensor(f"r{role}_psq1", [1, 512], F32) as pq1,
              nc.psum_tensor(f"r{role}_pbc", [128, 512], F32) as p_bc,
              nc.psum_tensor(f"r{role}_prot", [128, 512], F32) as p_rot):
            chl = _chunks(role)
            if KDBG.startswith("C"):
                chl = chl[:int(KDBG[1:]) + 1]
            for (c0, c1, is_own) in chl:
                if _chunk(e, role, c0, c1, is_own, s_B1, s_B2, s_kT, s_vT,
                          p_g, [ps0, ps1], [pq0, pq1], p_bc, p_rot, own0, keyw):
                    return
            if KDBG.startswith("C"):
                c0, c1, _ = chl[-1]
                dump3d(s_B2[:, 0:32 * (c1 - c0)].rearrange("p (k t) -> p k t", k=32),
                       c1 - c0)
                return

        if KDBG == "KV":
            gg = nc.gpsimd
            kw = min(keyw, 1280)
            e.dg.inc(gg.dma_start(e.d["y"][0:128, 0:kw], s_kT[:, 0:kw]), 16)
            e.dg.inc(gg.dma_start(e.d["y"][128:256, 0:kw], s_vT[:, 0:kw]), 16)
            gg.wait_ge(e.dg.h, e.dg.n)
            barrier(nc)
            return
        with nc.sbuf_tensor(f"r{role}_X", [128, 32 * ownw], BF16) as s_X:
            with nc.psum_tensor(f"r{role}_prot2", [128, 512], F32) as p_rot2:
                _qproj(e, role, s_X, p_g, p_rot2, own0, ownw)
            if KDBG == "Q":
                dump3d(e.s["qs"][:, :, 0:ownw].rearrange("h p t -> p h t"), ownw)
                return
            _attention(e, role, s_X, s_kT, s_vT, p_g, own0, ownw, keyw)
            if KDBG == "AT":
                dump3d(s_X[:, 0:32 * ownw].rearrange("p (k t) -> p k t", k=32), ownw)
                return
            _dense(e, role, s_X, p_g, own0, ownw)
            if KDBG == "DN":
                dump3d(e.s["r2s"][:, :, 0:ownw], ownw)
                return

        with (nc.sbuf_tensor(f"r{role}_G", [128, 32 * 640], BF16) as s_G,
              nc.psum_tensor(f"r{role}_f_psum0", [1, 512], F32) as fs0,
              nc.psum_tensor(f"r{role}_f_psum1", [1, 512], F32) as fs1,
              nc.psum_tensor(f"r{role}_f_psq0", [1, 512], F32) as fq0,
              nc.psum_tensor(f"r{role}_f_psq1", [1, 512], F32) as fq1,
              nc.psum_tensor(f"r{role}_f_pbc", [128, 512], F32) as f_bc):
            _ffn2(e, role, s_G, p_g, [fs0, fs1], [fq0, fq1], f_bc, own0, ownw)

        nc.sync.wait_ge(e.os.h, e.os.n)
        barrier(nc)


def _chunk(e, role, c0, c1, is_own, s_B1, s_B2, s_kT, s_vT,
           p_g, p_sum, p_sq, p_bc, p_rot, own0, keyw):
    nc = e.nc
    g, v, sc, t, sy = nc.gpsimd, nc.vector, nc.scalar, nc.tensor, nc.sync
    w = c1 - c0
    tcs = _tc(w)

    e.dg.inc(g.dma_start(s_B1[:, 0:32 * w].rearrange("p (k t) -> p k t", k=32),
                         e.d["xT"][:, c0:c1].rearrange("(k p) t -> p k t", p=128)), 16)
    for k, kk in (("cos", "cosk"), ("sin", "sink")):
        e.dg.inc(g.dma_start(e.s_tab[k][:, 0:w], e.d[kk][:, c0:c1]), 16)
    g.wait_ge(e.dg.h, e.dg.n)
    barrier(nc)
    if KDBG == "S1":
        e.dg.inc(g.dma_start(e.d["y"][0:1, 0:1], e.s_par[0:1, 0:1].bitcast(F32)[:, 0:1]), 16)
        g.wait_ge(e.dg.h, e.dg.n)
        barrier(nc)
        return True

    _stats(e, lambda kt: s_B1[:, kt * w:kt * w + w], w, p_sum, p_sq)
    if KDBG == "S2":
        e.dg.inc(g.dma_start(e.d["y"][0:1, 0:1], e.s_par[0:1, 0:1].bitcast(F32)[:, 0:1]), 16)
        g.wait_ge(e.dg.h, e.dg.n)
        barrier(nc)
        return True
    _bcast(e, e.s_Rb, e.s_row["rstd"][:, 0:w], w, p_bc, scale=0.5)
    if KDBG == "S3":
        e.dg.inc(g.dma_start(e.d["y"][0:1, 0:1], e.s_par[0:1, 0:1].bitcast(F32)[:, 0:1]), 16)
        g.wait_ge(e.dg.h, e.dg.n)
        barrier(nc)
        return True

    def epi1(i, c, a, b, pap, bs):
        v.wait_ge(e.ss.h, e.ss.n)
        e.vv.inc(v.tensor_tensor(e.s_t1[bs][:, 0:b - a], pap, e.s_Rb[:, a:b], A.mult))
        e.pf.inc(v.nop())
        sc.wait_ge(e.vv.h, e.vv.n)
        e.ss.inc(sc.activation(e.s_t2[bs][:, 0:b - a], e.s_t1[bs][:, 0:b - a],
                               AF.Identity, bias=e.s_bffn[:, i:i + 1]))
        v.wait_ge(e.ss.h, e.ss.n)
        e.vv.inc(v.tensor_tensor(s_B2[:, i * w + a:i * w + b],
                                 s_B1[:, i * w + a:i * w + b],
                                 e.s_t2[bs][:, 0:b - a], A.add))

    gemm(e, n_out=32,
         rhs=lambda kt, a, b: s_B1[:, kt * w + a:kt * w + b],
         wsrc=lambda i: e.s["wt2d"][:, i * 128:(i + 1) * 128]
             .rearrange("(k p) o -> p k o", p=128),
         weng="s", p_g=p_g, tcs=tcs,
         rank1=(lambda i: e.s_ncol[0:1, i * 128:(i + 1) * 128],
                lambda a, b: e.s_mbf[0:1, a:b]),
         epilogue=epi1)

    if is_own:
        e.dg.inc(g.dma_start(e.s["r1s"][:, :, c0 - own0:c1 - own0],
                             s_B2[:, 0:32 * w].rearrange("p (k t) -> p k t", k=32)), 16)
        g.wait_ge(e.dg.h, e.dg.n)
        barrier(nc)

    _stats(e, lambda kt: s_B2[:, kt * w:kt * w + w], w, p_sum, p_sq)
    _bcast(e, e.s_M2, e.s_row["m"][:, 0:w], w, p_bc)
    _bcast(e, e.s_R2, e.s_row["rstd"][:, 0:w], w, p_bc)
    for kt in range(32):
        e.vv.inc(v.tensor_tensor(e.s_mv[:, 0:w], s_B2[:, kt * w:kt * w + w],
                                 e.s_M2[:, 0:w], A.subtract))
        v.wait_ge(e.vv.h, e.vv.n)
        e.vv.inc(v.tensor_tensor(s_B1[:, kt * w:kt * w + w], e.s_mv[:, 0:w],
                                 e.s_R2[:, 0:w], A.mult))
        v.wait_ge(e.vv.h, e.vv.n)
    barrier(nc)
    if is_own:
        e.dg.inc(g.dma_start(e.s["xns"][:, :, c0 - own0:c1 - own0],
                             s_B1[:, 0:32 * w].rearrange("p (k t) -> p k t", k=32)), 16)
        g.wait_ge(e.dg.h, e.dg.n)
        barrier(nc)

    rotv = {"n": 0}

    def epi_kv(i, c, a, b, pap, bs):
        g_ = i % 8
        bcol = g_ * 6 + (4 if i < 8 else 5)
        e.ss.inc(sc.activation(e.s_t2[bs][:, 0:b - a], pap, AF.Identity,
                               bias=e.s_bqkv[:, bcol:bcol + 1]))
        e.pf.inc(sc.nop())
        v.wait_ge(e.ss.h, e.ss.n)
        v.wait_ge(e.tt.h, e.tt.n)
        e.vv.inc(v.tensor_copy(e.s_tb[bs][:, 0:b - a], e.s_t2[bs][:, 0:b - a]))
        if i < 8:
            t.wait_ge(e.vv.h, e.vv.n)
            if rotv["n"]:
                t.wait_ge(e.vv.h, rotv["n"])
            e.tt.inc(t.matmul(p_rot[:, 0:b - a], e.s_pmat[:, :], e.s_tb[bs][:, 0:b - a],
                              start=True, stop=True, skip_group_check=True))
            e.vv.inc(v.tensor_tensor(e.s_m1[:, 0:b - a], e.s_tb[bs][:, 0:b - a],
                                     e.s_tab["cos"][:, a:b], A.mult))
            v.wait_ge(e.tt.h, e.tt.n)
            e.vv.inc(v.tensor_tensor(e.s_mv[:, 0:b - a], p_rot[:, 0:b - a],
                                     e.s_tab["sin"][:, a:b], A.mult))
            rotv["n"] = e.vv.n
            v.wait_ge(e.vv.h, e.vv.n)
            e.vv.inc(v.tensor_tensor(
                s_kT[:, g_ * keyw + c0 + a:g_ * keyw + c0 + b],
                e.s_m1[:, 0:b - a], e.s_mv[:, 0:b - a], A.add))
        else:
            t.wait_ge(e.vv.h, e.vv.n)
            for j in range(0, b - a, 128):
                jw = min(128, b - a - j)
                ktile = (c0 + a + j) // 128
                e.tt.inc(t.transpose(p_rot[0:jw, 0:128].bitcast(BF16)[:, 0:128],
                                     e.s_tb[bs][:, j:j + jw], e.s_ident[:, :]))
                sc.wait_ge(e.tt.h, e.tt.n)
                e.ss.inc(sc.copy(
                    s_vT[0:jw, g_ * keyw + ktile * 128:g_ * keyw + ktile * 128 + 128],
                    p_rot[0:jw, 0:128].bitcast(BF16)[:, 0:128]))
                t.wait_ge(e.ss.h, e.ss.n)

    def wsrc_kv(i):
        g_ = i % 8
        col = g_ * 768 + (512 if i < 8 else 640)
        return e.d["wqkvT"][:, col:col + 128].rearrange("(k p) o -> p k o", p=128)

    gemm(e, n_out=16,
         rhs=lambda kt, a, b: s_B1[:, kt * w + a:kt * w + b],
         wsrc=wsrc_kv, weng="g", p_g=p_g, tcs=tcs, epilogue=epi_kv)


def _qproj(e, role, s_X, p_g, p_rot, own0, ownw):
    nc = e.nc
    g, v, sc, t = nc.gpsimd, nc.vector, nc.scalar, nc.tensor
    e.dg.inc(g.dma_start(s_X[:, 0:32 * ownw].rearrange("p (k t) -> p k t", k=32),
                         e.s["xns"][:, :, 0:ownw]), 16)
    g.wait_ge(e.dg.h, e.dg.n)
    barrier(nc)

    for (a0, b0) in _tc(ownw):
        w = b0 - a0
        for k, kk in (("cos", "cosq"), ("sin", "sinq")):
            e.dg.inc(g.dma_start(e.s_tab[k][:, 0:w], e.d[kk][:, own0 + a0:own0 + b0]), 16)
        g.wait_ge(e.dg.h, e.dg.n)
        barrier(nc)
        rotv = {"n": 0}

        def epi_q(i, c, a, b, pap, bs, a0=a0):
            bcol = (i // 4) * 6 + (i % 4)
            e.ss.inc(sc.activation(e.s_t2[bs][:, 0:b - a], pap, AF.Identity,
                                   bias=e.s_bqkv[:, bcol:bcol + 1]))
            e.pf.inc(sc.nop())
            v.wait_ge(e.ss.h, e.ss.n)
            v.wait_ge(e.tt.h, e.tt.n)
            v.wait_ge(e.dg.h, e.dg.n)
            e.vv.inc(v.tensor_copy(e.s_tb[bs][:, 0:b - a], e.s_t2[bs][:, 0:b - a]))
            t.wait_ge(e.vv.h, e.vv.n)
            e.tt.inc(t.matmul(p_rot[:, 0:b - a], e.s_pmat[:, :], e.s_tb[bs][:, 0:b - a],
                              start=True, stop=True, skip_group_check=True))
            e.vv.inc(v.tensor_tensor(e.s_m1[:, 0:b - a], e.s_tb[bs][:, 0:b - a],
                                     e.s_tab["cos"][:, a:b], A.mult))
            v.wait_ge(e.tt.h, e.tt.n)
            e.vv.inc(v.tensor_tensor(e.s_mv[:, 0:b - a], p_rot[:, 0:b - a],
                                     e.s_tab["sin"][:, a:b], A.mult))
            rotv["n"] = e.vv.n
            v.wait_ge(e.vv.h, e.vv.n)
            e.vv.inc(v.tensor_tensor(e.s_m1[:, 0:b - a], e.s_m1[:, 0:b - a],
                                     e.s_mv[:, 0:b - a], A.add))
            g.wait_ge(e.vv.h, e.vv.n)
            e.dg.inc(g.dma_start(e.s["qs"][i, :, a0 + a:a0 + b], e.s_m1[:, 0:b - a]), 16)

        gemm(e, n_out=32,
             rhs=lambda kt, a, b, a0=a0: s_X[:, kt * ownw + a0 + a:kt * ownw + a0 + b],
             wsrc=lambda i: e.d["wqkvT"][:, (i // 4) * 768 + (i % 4) * 128:
                                         (i // 4) * 768 + (i % 4) * 128 + 128]
                 .rearrange("(k p) o -> p k o", p=128),
             weng="g", p_g=p_g, tcs=[(0, w)], epilogue=epi_q)
    g.wait_ge(e.dg.h, e.dg.n)
    barrier(nc)


def _attention(e, role, s_X, s_kT, s_vT, p_g, own0, ownw, keyw):
    nc = e.nc
    g, v, sc, t, sy = nc.gpsimd, nc.vector, nc.scalar, nc.tensor, nc.sync
    own1 = own0 + ownw
    qtiles = [(a, min(a + 128, own1)) for a in range(own0, own1, 128)]

    with (
        nc.psum_tensor(f"r{role}_pctx", [128, 128], F32) as p_ctx,
        nc.psum_tensor(f"r{role}_pzb", [128, 128], F32) as p_zb,
    ):
        unit = 0
        wcnt = 0
        ex_tgt = {}
        ctx_tgt = {}
        evac_tgt = {}
        zb_tgt = {}
        ze_tgt = {}
        qfree_tgt = {}
        for (q0, q1) in qtiles:
            qw = q1 - q0
            qb_lo, qb_hi = q0 // BLOCK, q1 // BLOCK
            union = sorted(set().union(*[set(_blist(qb))
                                         for qb in range(qb_lo, qb_hi)]))
            tiles = sorted({b // 2 for b in union})
            groups = [(2 * tl, 2, 0) for tl in tiles]
            ng = len(groups)
            nw = (ng + 3) // 4
            for h in range(NH):
                g_ = h // 4
                ub = unit % 2
                if unit >= KATN:
                    unit += 1
                    continue
                if unit >= 2:
                    sy.wait_ge(e.tt.h, qfree_tgt[unit - 2])
                e.qd.inc(sy.dma_start(e.s_qh[ub][:, 0:qw],
                                      e.s["qs"][h, :, q0 - own0:q1 - own0]), 16)
                t.wait_ge(e.qd.h, e.qd.n)
                for wv in range(nw):
                    bank = wcnt % 2
                    if wcnt >= 2:
                        t.wait_ge(e.ss.h, ex_tgt[wcnt - 2])
                    gs = groups[wv * 4:wv * 4 + 4]
                    for si, (b0, nb, ro) in enumerate(gs):
                        mm = t.matmul(
                            p_g[bank][ro:ro + 64 * nb, si * 128:si * 128 + qw],
                            s_kT[:, g_ * keyw + b0 * 64:g_ * keyw + (b0 + nb) * 64],
                            e.s_qh[ub][:, 0:qw],
                            start=True, stop=True, skip_group_check=True)
                    e.tt.inc(mm)
                    v.wait_ge(e.tt.h, e.tt.n)
                    for si, (b0, nb, ro) in enumerate(gs):
                        for kk in range(nb):
                            kb = b0 + kk
                            for qqb in range(qb_lo, qb_hi):
                                qo = (qqb - qb_lo) * 64
                                if _attended(qqb, kb):
                                    if kb != qqb:
                                        continue
                                    ap = p_g[bank][ro + kk * 64:ro + kk * 64 + 64,
                                                   si * 128 + qo:si * 128 + qo + 64]
                                    e.vv.inc(v.tensor_tensor(
                                        ap, ap, e.s_mdiag[0:64, 0:64], A.add))
                                else:
                                    ap = p_g[bank][ro + kk * 64:ro + kk * 64 + 64,
                                                   si * 128 + qo:si * 128 + qo + 64]
                                    e.vv.inc(v.tensor_scalar(
                                        ap, ap, -MASK_NEG, None, A.add))
                    sc.wait_ge(e.vv.h, e.vv.n)
                    if unit >= 2 and wv == 0 and (unit - 2) in ctx_tgt:
                        sc.wait_ge(e.tt.h, ctx_tgt[unit - 2])
                    wcols = len(gs) * 128
                    e.ss.inc(sc.activation(
                        e.s_probs[ub][:, wv * 512:wv * 512 + wcols],
                        p_g[bank][:, 0:wcols], AF.Exp, bias=-EXP_SHIFT))
                    ex_tgt[wcnt] = e.ss.n
                    wcnt += 1
                qfree_tgt[unit] = e.tt.n
                if KATM == "sc":
                    unit += 1
                    continue
                t.wait_ge(e.ss.h, ex_tgt[wcnt - 1])
                if unit >= 1:
                    t.wait_ge(e.vv.h, evac_tgt[unit - 1])
                first = True
                for gi, (b0, nb, ro) in enumerate(groups):
                    mmz = t.matmul(p_zb[:, 0:qw], e.s_onesm[:, :],
                             e.s_probs[ub][0:128,
                                           (gi // 4) * 512 + (gi % 4) * 128:
                                           (gi // 4) * 512 + (gi % 4) * 128 + qw],
                             start=first, stop=(gi == ng - 1), skip_group_check=True)
                    first = False
                e.tt.inc(mmz)
                z_tgt = e.tt.n
                first = True
                for gi, (b0, nb, ro) in enumerate(groups):
                    ktile = b0 // 2
                    mmc = t.matmul(p_ctx[:, 0:qw],
                             s_vT[0:128,
                                  g_ * keyw + ktile * 128:g_ * keyw + ktile * 128 + 128],
                             e.s_probs[ub][0:128,
                                           (gi // 4) * 512 + (gi % 4) * 128:
                                           (gi // 4) * 512 + (gi % 4) * 128 + qw],
                             start=first, stop=(gi == ng - 1), skip_group_check=True)
                    first = False
                e.tt.inc(mmc)
                ctx_tgt[unit] = e.tt.n
                if KATM == "ctx":
                    v.wait_ge(e.tt.h, ctx_tgt[unit])
                    e.vv.inc(v.tensor_copy(
                        s_X[:, h * ownw + (q0 - own0):h * ownw + (q1 - own0)],
                        p_ctx[:, 0:qw]))
                    evac_tgt[unit] = e.vv.n
                    unit += 1
                    continue
                v.wait_ge(e.tt.h, z_tgt)
                if unit >= 2:
                    v.wait_ge(e.vv.h, evac_tgt[unit - 2])
                e.vv.inc(v.reciprocal(e.s_zb[ub][:, 0:qw], p_zb[:, 0:qw]))
                v.wait_ge(e.vv.h, e.vv.n)
                v.wait_ge(e.tt.h, ctx_tgt[unit])
                e.vv.inc(v.tensor_tensor(
                    s_X[:, h * ownw + (q0 - own0):h * ownw + (q1 - own0)],
                    p_ctx[:, 0:qw], e.s_zb[ub][:, 0:qw], A.mult))
                evac_tgt[unit] = e.vv.n
                unit += 1
        barrier(nc)


def _dense(e, role, s_X, p_g, own0, ownw):
    nc = e.nc
    g, v, sc, sy = nc.gpsimd, nc.vector, nc.scalar, nc.sync
    tcs = _tc(ownw)

    def epi_d(i, c, a, b, pap, bs):
        e.ss.inc(sc.activation(e.s_t2[bs][:, 0:b - a], pap, AF.Identity,
                               bias=e.s_bd[:, i:i + 1]))
        e.pf.inc(sc.nop())
        sy.wait_ge(e.vv.h, e.vv.n)
        e.dy.inc(sy.dma_start(e.s_rr[bs][:, 0:b - a], e.s["r1s"][:, i, a:b]), 16)
        v.wait_ge(e.ss.h, e.ss.n)
        v.wait_ge(e.dy.h, e.dy.n)
        e.vv.inc(v.tensor_tensor(e.s_out[bs][:, 0:b - a], e.s_rr[bs][:, 0:b - a],
                                 e.s_t2[bs][:, 0:b - a], A.add))
        v.wait_ge(e.vv.h, e.vv.n)
        e.vv.inc(v.tensor_copy(e.s_tb[bs][:, 0:b - a], e.s_out[bs][:, 0:b - a]))
        g.wait_ge(e.vv.h, e.vv.n)
        e.dg.inc(g.dma_start(e.s["r2s"][:, i, a:b], e.s_tb[bs][:, 0:b - a]), 16)

    gemm(e, n_out=32,
         rhs=lambda kt, a, b: s_X[:, kt * ownw + a:kt * ownw + b],
         wsrc=lambda i: e.d["wdT"][:, i * 128:(i + 1) * 128]
             .rearrange("(k p) o -> p k o", p=128),
         weng="g", p_g=p_g, tcs=tcs, epilogue=epi_d)
    g.wait_ge(e.dg.h, e.dg.n)
    barrier(nc)


def _ffn2(e, role, s_B2, p_g, p_sum, p_sq, p_bc, own0, ownw):
    nc = e.nc
    g, v, sc, sy = nc.gpsimd, nc.vector, nc.scalar, nc.sync
    own1 = own0 + ownw
    groups = []
    a = own0
    while a < own1:
        groups.append((a, min(a + 640, own1)))
        a += 640
    for (g0, g1) in groups:
        w = g1 - g0
        tcs = _tc(w)
        e.dy.inc(sy.dma_start(
            s_B2[:, 0:32 * w].rearrange("p (k t) -> p k t", k=32),
            e.s["r2s"][:, :, g0 - own0:g1 - own0]), 16)
        sy.wait_ge(e.dy.h, e.dy.n)
        barrier(nc)
        _stats(e, lambda kt: s_B2[:, kt * w:kt * w + w], w, p_sum, p_sq)
        _bcast(e, e.s_Rb, e.s_row["rstd"][:, 0:w], w, p_bc, scale=0.5)

        def epi2(i, c, a, b, pap, bs, g0=g0, w=w):
            v.wait_ge(e.ss.h, e.ss.n)
            v.wait_ge(e.os.h, e.os.n)
            e.vv.inc(v.tensor_tensor(e.s_t1[bs][:, 0:b - a], pap, e.s_Rb[:, a:b],
                                     A.mult))
            e.pf.inc(v.nop())
            sc.wait_ge(e.vv.h, e.vv.n)
            e.ss.inc(sc.activation(e.s_t2[bs][:, 0:b - a], e.s_t1[bs][:, 0:b - a],
                                   AF.Identity, bias=e.s_bffn[:, i:i + 1]))
            v.wait_ge(e.ss.h, e.ss.n)
            e.vv.inc(v.tensor_tensor(e.s_out[bs][:, 0:b - a],
                                     s_B2[:, i * w + a:i * w + b],
                                     e.s_t2[bs][:, 0:b - a], A.add))
            sy.wait_ge(e.vv.h, e.vv.n)
            e.os.inc(sy.dma_start(
                e.d["y"][i * 128:(i + 1) * 128, g0 - own0 + a:g0 - own0 + b],
                e.s_out[bs][:, 0:b - a]), 16)

        gemm(e, n_out=32,
             rhs=lambda kt, a, b, w=w: s_B2[:, kt * w + a:kt * w + b],
             wsrc=lambda i: e.s["wt2d"][:, i * 128:(i + 1) * 128]
                 .rearrange("(k p) o -> p k o", p=128),
             weng="s", p_g=p_g, tcs=tcs,
             rank1=(lambda i: e.s_ncol[0:1, i * 128:(i + 1) * 128],
                    lambda a, b: e.s_mbf[0:1, a:b]),
             epilogue=epi2)



def _host_inputs(inputs):
    x = np.asarray(inputs["L0_attn_hidden_states"], dtype=np.float32)
    w_qkv = np.asarray(inputs["L0_attn_w_qkv"], dtype=np.float32)
    b_qkv = np.asarray(inputs["L0_attn_b_qkv"], dtype=np.float32)
    w_dense = np.asarray(inputs["L0_attn_w_dense"], dtype=np.float32)
    b_dense = np.asarray(inputs["L0_attn_b_dense"], dtype=np.float32)
    packed = np.asarray(inputs["L0_ffn_packed_weight"])
    scale = np.asarray(inputs["L0_ffn_scale"], dtype=np.float32)
    b_ffn = np.asarray(inputs["L0_ffn_bias"], dtype=np.float32)

    sh = {}
    sh["pwT"] = np.ascontiguousarray(packed[PERM].T).astype(np.int32)
    sh["scaleT"] = np.ascontiguousarray(scale[PERM].T)
    sh["wqkvT"] = np.ascontiguousarray(w_qkv.T[PERM, :])
    sh["wdT"] = np.ascontiguousarray(w_dense.T[:, PERM])
    sh["bffn"] = np.ascontiguousarray(b_ffn[PERM].reshape(32, 128).T)
    sh["bqkv"] = np.ascontiguousarray(b_qkv.reshape(48, 128).T)
    sh["bd"] = np.ascontiguousarray(b_dense[PERM].reshape(32, 128).T)

    invf = 1.0 / (1000000.0 ** (np.arange(0, HD, 2, dtype=np.float64) / HD))
    tpos = np.arange(SEQ, dtype=np.float64)
    fr = np.outer(tpos, invf)
    emb = np.concatenate([fr, fr], axis=1)
    cosT, sinT = np.cos(emb).T, np.sin(emb).T
    sh["cosq"] = np.ascontiguousarray((cosT / HD)).astype(BF16NP)
    sh["sinq"] = np.ascontiguousarray((sinT / HD)).astype(BF16NP)
    sh["cosk"] = np.ascontiguousarray(cosT).astype(BF16NP)
    sh["sink"] = np.ascontiguousarray(sinT).astype(BF16NP)

    pm = np.zeros((128, 128), dtype=np.float32)
    for dd in range(64):
        pm[dd + 64, dd] = -1.0
        pm[dd, dd + 64] = 1.0
    sh["pmat"] = pm.astype(BF16NP)
    sh["ident"] = np.eye(128, dtype=BF16NP)
    kk = np.arange(128)[:, None]
    qq = np.arange(128)[None, :]
    sh["mdiag"] = np.where(kk <= qq, 0.0, -MASK_NEG).astype(np.float32)
    sh["onesc"] = np.ones((128, 1), dtype=BF16NP)
    sh["onesr"] = np.ones((1, 128), dtype=BF16NP)
    sh["onesrf"] = np.ones((1, 128), dtype=np.float32)
    sh["onesm"] = np.ones((128, 128), dtype=BF16NP)

    in_maps = []
    for c in range(8):
        b, r = divmod(c, 2)
        m = dict(sh)
        m["xT"] = np.ascontiguousarray(x[b].T[PERM, :])
        m["par"] = np.array([[r]], dtype=np.int32)
        in_maps.append(m)
    return in_maps


def kernel(**inputs):
    in_maps = _host_inputs(inputs)
    nc = build_program()
    res = run_bass_kernel_spmd(nc, in_maps, core_ids=list(range(8)))
    out = np.zeros((4, SEQ, HID), dtype=np.float32)
    for c in range(8):
        b, r = divmod(c, 2)
        yv = np.asarray(res.results[c]["y"])
        if r == 0:
            out[b, 0:TSPLIT, :][:, PERM] = yv[:, 0:TSPLIT].T
        else:
            out[b, TSPLIT:SEQ, :][:, PERM] = yv[:, 0:SEQ - TSPLIT].T
    return out
